# version 11
# baseline (speedup 1.0000x reference)
import numpy as np
import ml_dtypes
import concourse.bass as bass
import concourse.mybir as mybir
from concourse.bass_utils import run_bass_kernel_spmd
from contextlib import ExitStack

F32 = mybir.dt.float32
BF16 = mybir.dt.bfloat16
FP8 = mybir.dt.float8e4
AF = mybir.ActivationFunctionType
ALU = mybir.AluOpType
AX = mybir.AxisListType

NCORES = 8
BPC = 4
T = 2048
C = 1024
H = 16
DH = 64
WIN = 512
TT = 512
NTL = T // TT
QT = 128
NQT = TT // QT

ENGS = ("pe", "act", "dve", "sp", "gp")


class Op:
    __slots__ = ("eng", "cls", "fn", "deps", "idx", "rank", "marked")

    def __init__(self, eng, fn, deps, idx, cls):
        self.eng = eng
        self.cls = cls
        self.fn = fn
        self.deps = deps
        self.idx = idx
        self.rank = 0
        self.marked = False


class Buf:
    __slots__ = ("w", "r")

    def __init__(self):
        self.w = None
        self.r = []


class Prog:
    def __init__(self):
        self.ops = {e: [] for e in ENGS}
        self.prelude_last = {}
        self.scratch = None

    def mark_prelude(self):
        for e in ("sp", "gp"):
            for op in self.ops[e]:
                self.prelude_last[op.cls] = op

    def add(self, eng, fn, reads=(), writes=(), extra=(), cls=None):
        deps = []
        for b in reads:
            if b.w is not None:
                deps.append(b.w)
        for b in writes:
            if b.w is not None:
                deps.append(b.w)
            deps.extend(b.r)
        deps.extend(extra)
        op = Op(eng, fn, deps, len(self.ops[eng]), cls or eng)
        self.ops[eng].append(op)
        for b in reads:
            b.r.append(op)
        for b in writes:
            b.w = op
            b.r = []
        return op

    def finalize(self):
        for e in ("sp", "gp"):
            for op in self.ops[e]:
                op.marked = True
        for e in ENGS:
            for op in self.ops[e]:
                for d in op.deps:
                    if d.eng != op.eng:
                        d.marked = True
        ranks = {}
        for e in ENGS:
            for op in self.ops[e]:
                if op.marked:
                    ranks[op.cls] = ranks.get(op.cls, 0) + 1
                    op.rank = ranks[op.cls]

    def replay(self, eng_name, eng, sems):
        maxw = {}
        for op in self.ops[eng_name]:
            for d in op.deps:
                if d.eng == eng_name and d.eng not in ("sp", "gp"):
                    continue
                if d.cls in self.prelude_last:
                    last = self.prelude_last[d.cls]
                    if d.idx <= last.idx:
                        d = last
                v = d.rank * (16 if d.eng in ("sp", "gp") else 1)
                if v > maxw.get(d.cls, 0):
                    eng.wait_ge(sems[d.cls], v)
                    maxw[d.cls] = v
            inst = op.fn(eng)
            if op.marked:
                if op.eng == "dve":
                    inst = eng.tensor_copy(self.scratch[0:1, 8:9],
                                           self.scratch[0:1, 9:10])
                elif op.eng == "act":
                    inst = eng.copy(self.scratch[0:1, 0:1],
                                    self.scratch[0:1, 1:2])
                inst.then_inc(sems[op.cls], 16 if op.eng in ("sp", "gp") else 1)


def band_chunks(tt, qt):
    q0 = tt * TT + qt * QT
    start = max(0, q0 - WIN)
    return list(range(start, q0 + QT, QT)), q0, start


def build_program():
    nc = bass.Bass("TRN2", target_bir_lowering=False)
    ap = bass.AP

    xT = nc.dram_tensor("xT", [BPC * C, T], F32, kind="ExternalInput")
    wq_d = nc.dram_tensor("wqT", [C, C], F32, kind="ExternalInput")
    wk_d = nc.dram_tensor("wkT", [C, C], F32, kind="ExternalInput")
    wv_d = nc.dram_tensor("wvT", [C, C], F32, kind="ExternalInput")
    wo_d = nc.dram_tensor("woT", [C, C], F32, kind="ExternalInput")
    wf_d = nc.dram_tensor("wfT", [C, C], F32, kind="ExternalInput")
    bias_d = nc.dram_tensor("biasT", [128, 8], F32, kind="ExternalInput")
    idn_d = nc.dram_tensor("idn", [128, 128], F32, kind="ExternalInput")
    msk_d = nc.dram_tensor("masks", [128, 256], BF16, kind="ExternalInput")
    out_d = nc.dram_tensor("out", [BPC * T, C], F32, kind="ExternalOutput")

    st = ExitStack()
    sb = lambda name, shape, dt: st.enter_context(nc.sbuf_tensor(name, shape, dt))
    psum = lambda name, shape, dt: st.enter_context(nc.psum_tensor(name, shape, dt))

    wq = sb("wq", [128, 8192], BF16)
    wk = sb("wk", [128, 8192], BF16)
    wv = sb("wv", [128, 8192], BF16)
    wo = sb("wo", [128, 8192], BF16)
    wf8 = sb("wf8", [128, 8192], FP8)
    biasT = sb("biasTs", [128, 8], F32)
    idn = sb("idns", [128, 128], F32)
    idnb = sb("idnb", [128, 128], BF16)
    masks = sb("maskss", [128, 256], BF16)
    rowsC = sb("rowsC", [128, 8], F32)
    x32 = [sb("x32a", [128, 4096], F32), sb("x32b", [128, 4096], F32)]
    xb = sb("xb", [128, 4096], BF16)
    Qb = sb("Qb", [128, 8192], BF16)
    Kr = [sb("k0", [128, 4096], BF16), sb("k1", [128, 4096], BF16)]
    Vr = [sb("v0", [128, 4160], BF16), sb("v1", [128, 4160], BF16)]
    Pt = [sb("p0", [128, 640], BF16), sb("p1", [128, 640], BF16)]
    Ofm = sb("ofm", [128, 4096], BF16)
    Otm = xb
    X1 = sb("x1tm", [128, 4096], F32)
    Q8 = sb("q8", [128, 4096], FP8)
    OutS = [sb("os0", [128, 1024], F32), sb("os1", [128, 1024], F32)]
    Sq = sb("sq", [128, 1024], BF16)
    Rws = sb("rws", [128, 64], F32)
    Scrp = sb("scrp", [128, 16], F32)
    Rinv = sb("rinv", [128, 64], F32)

    rot = [psum("rot0", [128, 512], F32), psum("rot1", [128, 512], F32)]
    S0 = psum("s0", [128, 1024], F32)
    S1 = psum("s1", [128, 1024], F32)
    LNt = psum("lnt", [128, 1024], F32)

    P = Prog()
    b_w = {n: Buf() for n in ("wq", "wk", "wv", "wo", "wf8", "biasT", "idn",
                              "idnb", "masks", "rowsC")}
    b_x32 = [Buf(), Buf()]
    b_xb = Buf()
    b_Otm = b_xb
    b_Q = Buf()
    b_K = [Buf(), Buf()]
    b_V = [Buf(), Buf()]
    b_Pt = [Buf(), Buf()]
    b_Ofm = Buf()
    b_X1 = Buf()
    b_Q8 = Buf()
    b_OutS = [Buf(), Buf()]
    b_Sq = Buf()
    b_Rws = Buf()
    b_Rinv = Buf()
    b_rot = [Buf(), Buf()]
    b_S = [Buf(), Buf()]
    b_LNt = Buf()

    WSB = {"q": wq, "k": wk, "v": wv, "o": wo}

    for nm, dram in (("wq", wq_d), ("wk", wk_d), ("wv", wv_d), ("wo", wo_d)):
        dst = WSB[nm[1]]
        P.add("gp", lambda e, dst=dst, dram=dram: e.dma_start(
            out=dst[:, :],
            in_=ap(dram, 0, [[C, 128], [128 * C, 8], [1, C]])),
            writes=[b_w[nm]], cls="gp_pre")
    P.add("sp", lambda e: e.dma_start(out=biasT[:, :],
          in_=ap(bias_d, 0, [[8, 128], [1, 8]])), writes=[b_w["biasT"]],
          cls="sp_pre")
    P.add("sp", lambda e: e.dma_start(out=idn[:, :],
          in_=ap(idn_d, 0, [[128, 128], [1, 128]])), writes=[b_w["idn"]],
          cls="sp_pre")
    P.add("sp", lambda e: e.dma_start(out=masks[:, :],
          in_=ap(msk_d, 0, [[256, 128], [1, 256]])), writes=[b_w["masks"]],
          cls="sp_pre")
    P.add("dve", lambda e: e.tensor_copy(idnb[:, :], idn[:, :]),
          reads=[b_w["idn"]], writes=[b_w["idnb"]])
    for col, val in ((0, 448.0), (1, -448.0), (2, 1e-5), (3, 1.0 / 1024.0),
                     (4, -12.0), (5, -1.0)):
        P.add("dve", lambda e, col=col, val=val:
              e.memset(rowsC[:, col:col + 1], val), writes=[b_w["rowsC"]])
    P.add("dve", lambda e: e.memset(Qb[:, :], 0.0), writes=[b_Q])
    for fn in (AF.Exp, AF.Square, AF.Sqrt, AF.Identity):
        P.add("act", lambda e, fn=fn: e.activation(
            Scrp[:, 0:8], rowsC[:, 0:8], fn),
            reads=[b_w["rowsC"]])
    P.add("act", lambda e: e.copy(Scrp[:, 0:8], rowsC[:, 0:8]),
          reads=[b_w["rowsC"]])
    for sl in range(2):
        P.add("dve", lambda e, sl=sl: e.memset(
            ap(Vr[sl], 64, [[4160, 128], [1040, 4], [65, 16]]), 1.0),
            writes=[b_V[sl]])
    for half in range(2):
        P.add("sp", lambda e, half=half: e.dma_start(
            out=x32[half][:, :],
            in_=ap(wf_d, half * 512 * C, [[C, 128], [128 * C, 4], [1, C]])),
            writes=[b_x32[half]], cls="sp_pre")
        for kc in range(4):
            kk = half * 4 + kc
            P.add("dve", lambda e, half=half, kc=kc, kk=kk: e.tensor_scalar(
                wf8[:, kk * 1024:(kk + 1) * 1024],
                x32[half][:, kc * 1024:(kc + 1) * 1024],
                rowsC[:, 0:1], rowsC[:, 1:2], ALU.min, ALU.max),
                reads=[b_x32[half], b_w["rowsC"]], writes=[b_w["wf8"]])

    P.scratch = Scrp
    P.mark_prelude()

    n_out_dma = 0
    last_out_op = None
    for tile in range(BPC * NTL):
        s, tt = divmod(tile, NTL)
        p = tile % 2

        xoff = s * C * T + tt * TT
        P.add("sp", lambda e, xoff=xoff, p=p: e.dma_start(
            out=x32[p][:, :],
            in_=ap(xT, xoff, [[T, 128], [128 * T, 8], [1, TT]])),
            writes=[b_x32[p]], cls="sp_x")
        P.add("gp", lambda e, xoff=xoff: e.dma_start(
            out=xb[:, :],
            in_=ap(xT, xoff, [[T, 128], [128 * T, 8], [1, TT]])),
            writes=[b_xb], cls="gp_x")

        rot_i = [0]

        def claim():
            r = rot_i[0] & 1
            rot_i[0] += 1
            return r

        def dve_gap(n=3):
            for _ in range(n):
                P.add("dve", lambda e: e.tensor_copy(
                    Scrp[0:1, 10:11], Scrp[0:1, 11:12]))

        def spacer_t(bufs):
            P.add("pe", lambda e: e.transpose(
                S1[0:1, 768:896], idn[:, 0:1], idn[:, :]),
                reads=[b_w["idn"]], writes=[b_S[1]])
            spacer(bufs)

        def spacer(bufs):
            w = list(bufs)
            if b_S[1] not in w:
                w.append(b_S[1])
            P.add("pe", lambda e: e.matmul(
                S1[0:1, 1023:1024], idnb[:, 0:1], idnb[:, 0:1],
                start=True, stop=True), writes=w)

        for nm, dstbuf, dst in (("q", b_Q, Qb), ("k", b_K[tt % 2], Kr[tt % 2])):
            for j in range(8):
                r = claim()
                for kc in range(8):
                    P.add("pe", lambda e, r=r, nm=nm, j=j, kc=kc:
                          e.matmul(rot[r][:, :],
                                   WSB[nm][:, kc * 1024 + j * 128:
                                           kc * 1024 + j * 128 + 128],
                                   xb[:, kc * 512:(kc + 1) * 512],
                                   start=(kc == 0), stop=(kc == 7)),
                          reads=[b_w["w" + nm], b_xb], writes=[b_rot[r]])
                spacer([b_rot[r]])
                if nm == "k":
                    P.add("dve", lambda e, r=r, dst=dst, j=j: e.tensor_copy(
                        dst[:, j * 512:(j + 1) * 512], rot[r][:, :]),
                        reads=[b_rot[r]], writes=[dstbuf])
                else:
                    P.add("dve", lambda e, r=r, dst=dst, j=j: e.tensor_copy(
                        dst[0:64, (2 * j) * 512:(2 * j + 1) * 512],
                        rot[r][0:64, :]),
                        reads=[b_rot[r]], writes=[dstbuf])
                    P.add("dve", lambda e, r=r, dst=dst, j=j: e.tensor_copy(
                        dst[64:128, (2 * j + 1) * 512:(2 * j + 2) * 512],
                        rot[r][64:128, :]),
                        reads=[b_rot[r]], writes=[dstbuf])
        vsl = tt % 2
        for ts in range(4):
            for fh in range(2):
                r = claim()
                for kc in range(8):
                    P.add("pe", lambda e, r=r, ts=ts, fh=fh, kc=kc:
                          e.matmul(rot[r][:, :],
                                   xb[:, kc * 512 + ts * 128:
                                      kc * 512 + ts * 128 + 128],
                                   wv[:, kc * 1024 + fh * 512:
                                      kc * 1024 + fh * 512 + 512],
                                   start=(kc == 0), stop=(kc == 7)),
                          reads=[b_w["wv"], b_xb], writes=[b_rot[r]])
                spacer([b_rot[r]])
                P.add("dve", lambda e, r=r, vsl=vsl, ts=ts, fh=fh:
                      e.tensor_copy(
                          ap(Vr[vsl], ts * 1040 + fh * 520,
                             [[4160, 128], [65, 8], [1, 64]]),
                          rot[r][:, :]),
                      reads=[b_rot[r]], writes=[b_V[vsl]])

        for h in range(16):
            hb = (h % 2) * 64
            fc = h // 2
            pr = h % 2
            for qp in range(2):
                qts = [2 * qp, 2 * qp + 1]
                binfo = {qt: band_chunks(tt, qt) for qt in qts}
                union = sorted({k for qt in qts for k in binfo[qt][0]})
                for kabs in union:
                    ksl = (kabs // TT) % 2
                    kcol = kabs % TT
                    for qt in qts:
                        chunks, q0, bstart = binfo[qt]
                        if kabs not in chunks:
                            continue
                        crel = (kabs - bstart) // QT
                        sreg = S0 if qt % 2 == 0 else S1
                        sbuf_b = b_S[qt % 2]
                        P.add("pe", lambda e, ksl=ksl, kcol=kcol, h=h,
                              fc=fc, qt=qt, crel=crel, sreg=sreg:
                              e.matmul(sreg[:, crel * 128:(crel + 1) * 128],
                                       Kr[ksl][:, fc * 512 + kcol:
                                               fc * 512 + kcol + 128],
                                       Qb[:, h * 512 + qt * 128:
                                          h * 512 + qt * 128 + 128],
                                       start=True, stop=True),
                              reads=[b_K[ksl], b_Q],
                              writes=[sbuf_b])
                spacer([b_S[0], b_S[1]])
                for qt in qts:
                    chunks, q0, bstart = binfo[qt]
                    n = len(chunks)
                    sreg = S0 if qt % 2 == 0 else S1
                    P.add("act", lambda e, sreg=sreg, n=n, qt=qt:
                          e.activation(Pt[qt % 2][:, 0:n * 128],
                                       sreg[:, 0:n * 128], AF.Exp,
                                       bias=rowsC[:, 4:5], scale=0.125),
                          reads=[b_S[qt % 2], b_w["rowsC"]],
                          writes=[b_Pt[qt % 2]])
                    has_m0 = (q0 >= WIN)
                    if has_m0:
                        P.add("dve", lambda e, qt=qt, n=n: e.tensor_tensor(
                            ap(Pt[qt % 2], 0, [[640, 128], [(n - 1) * 128, 2],
                                               [1, 128]]),
                            ap(Pt[qt % 2], 0, [[640, 128], [(n - 1) * 128, 2],
                                               [1, 128]]),
                            ap(masks, 0, [[256, 128], [128, 2], [1, 128]]),
                            ALU.mult),
                            reads=[b_w["masks"]], writes=[b_Pt[qt % 2]])
                    else:
                        P.add("dve", lambda e, qt=qt, n=n: e.tensor_tensor(
                            Pt[qt % 2][:, (n - 1) * 128:n * 128],
                            Pt[qt % 2][:, (n - 1) * 128:n * 128],
                            masks[:, 128:256], ALU.mult),
                            reads=[b_w["masks"]], writes=[b_Pt[qt % 2]])
                for qt in qts:
                    chunks, q0, bstart = binfo[qt]
                    n = len(chunks)
                    for ci, kabs in enumerate(chunks):
                        ksl = (kabs // TT) % 2
                        ts = (kabs % TT) // 128
                        P.add("pe", lambda e, qt=qt, ci=ci, n=n, ksl=ksl,
                              ts=ts, h=h, pr=pr:
                              e.matmul(rot[pr][:, qt * 65:qt * 65 + 65],
                                       Pt[qt % 2][:, ci * 128:(ci + 1) * 128],
                                       Vr[ksl][:, ts * 1040 + h * 65:
                                               ts * 1040 + h * 65 + 65],
                                       start=(ci == 0), stop=(ci == n - 1)),
                              reads=[b_Pt[qt % 2], b_V[ksl]],
                              writes=[b_rot[pr]])
            P.add("dve", lambda e, pr=pr, h=h: e.reciprocal(
                Rinv[:, h * 4:h * 4 + 4],
                ap(rot[pr], 64, [[512, 128], [65, 4]])),
                reads=[b_rot[pr]], writes=[b_Rinv])
            spacer([b_rot[pr]])
            for qt in range(4):
                P.add("dve", lambda e, pr=pr, h=h, qt=qt: e.tensor_scalar_mul(
                    ap(Otm, qt * 1024 + h * 64, [[4096, 128], [1, 64]]),
                    ap(rot[pr], qt * 65, [[512, 128], [1, 64]]),
                    Rinv[:, h * 4 + qt:h * 4 + qt + 1]),
                    reads=[b_rot[pr], b_Rinv], writes=[b_Otm])

        for qt in range(4):
            for half in range(2):
                r = claim()
                for f4 in range(4):
                    fcc = half * 4 + f4
                    P.add("pe", lambda e, r=r, qt=qt, fcc=fcc, f4=f4:
                          e.matmul(rot[r][:, f4 * 128:(f4 + 1) * 128],
                                   Otm[:, qt * 1024 + fcc * 128:
                                       qt * 1024 + fcc * 128 + 128],
                                   idnb[:, :], start=True, stop=True),
                          reads=[b_Otm, b_w["idnb"]], writes=[b_rot[r]])
                spacer([b_rot[r]])
                P.add("dve", lambda e, r=r, qt=qt, half=half: e.tensor_copy(
                    ap(Ofm, half * 4 * 512 + qt * 128,
                       [[4096, 128], [512, 4], [1, 128]]),
                    rot[r][:, :]),
                    reads=[b_rot[r]], writes=[b_Ofm])

        for j in range(8):
            r = claim()
            for kc in range(8):
                P.add("pe", lambda e, r=r, j=j, kc=kc:
                      e.matmul(rot[r][:, :],
                               wo[:, kc * 1024 + j * 128:
                                  kc * 1024 + j * 128 + 128],
                               Ofm[:, kc * 512:(kc + 1) * 512],
                               start=(kc == 0), stop=(kc == 7)),
                      reads=[b_w["wo"], b_Ofm], writes=[b_rot[r]])
            spacer([b_rot[r]])
            P.add("dve", lambda e, r=r, j=j, p=p: e.tensor_add(
                x32[p][:, j * 512:(j + 1) * 512], rot[r][:, :],
                x32[p][:, j * 512:(j + 1) * 512]),
                reads=[b_rot[r]], writes=[b_x32[p]])

        for sub in range(4):
            for fcc in range(8):
                P.add("pe", lambda e, fcc=fcc, sub=sub, p=p: e.transpose(
                    LNt[:, fcc * 128:(fcc + 1) * 128],
                    x32[p][:, fcc * 512 + sub * 128:
                           fcc * 512 + sub * 128 + 128],
                    idn[:, :]),
                    reads=[b_x32[p], b_w["idn"]], writes=[b_LNt])
            spacer_t([b_LNt])
            c0 = sub * 8
            red_op = P.add("dve", lambda e, c0=c0: e.reduce_sum(
                Rws[:, c0:c0 + 1], LNt[:, :], axis=AX.X),
                reads=[b_LNt], writes=[b_Rws])
            P.add("act", lambda e, c0=c0: e.activation(
                Sq[:, :], LNt[:, :], AF.Square),
                reads=[b_LNt], writes=[b_Sq], extra=[red_op])
            P.add("dve", lambda e, c0=c0: e.reduce_sum(
                Rws[:, c0 + 1:c0 + 2], Sq[:, :], axis=AX.X),
                reads=[b_Sq], writes=[b_Rws])
            P.add("dve", lambda e, c0=c0: e.tensor_scalar_mul(
                Rws[:, c0 + 2:c0 + 3], Rws[:, c0:c0 + 1], rowsC[:, 3:4]),
                reads=[b_w["rowsC"]], writes=[b_Rws])
            dve_gap()
            P.add("dve", lambda e, c0=c0: e.tensor_mul(
                Rws[:, c0 + 5:c0 + 6], Rws[:, c0 + 2:c0 + 3],
                Rws[:, c0 + 2:c0 + 3]), writes=[b_Rws])
            dve_gap()
            P.add("dve", lambda e, c0=c0: e.scalar_tensor_tensor(
                Rws[:, c0 + 6:c0 + 7], Rws[:, c0 + 1:c0 + 2], rowsC[:, 3:4],
                Rws[:, c0 + 5:c0 + 6], ALU.mult, ALU.subtract),
                reads=[b_w["rowsC"]], writes=[b_Rws])
            P.add("act", lambda e, c0=c0: e.activation(
                Rws[:, c0 + 7:c0 + 8], Rws[:, c0 + 6:c0 + 7], AF.Sqrt,
                bias=rowsC[:, 2:3]),
                reads=[b_w["rowsC"]], writes=[b_Rws])
            P.add("dve", lambda e, c0=c0: e.reciprocal(
                Rws[:, c0 + 3:c0 + 4], Rws[:, c0 + 7:c0 + 8]),
                writes=[b_Rws])
            dve_gap()
            P.add("dve", lambda e, c0=c0: e.scalar_tensor_tensor(
                Rws[:, c0 + 4:c0 + 5], Rws[:, c0 + 2:c0 + 3], rowsC[:, 5:6],
                Rws[:, c0 + 3:c0 + 4], ALU.mult, ALU.mult),
                reads=[b_w["rowsC"]], writes=[b_Rws])
            P.add("act", lambda e, c0=c0, sub=sub: e.activation(
                X1[:, sub * 1024:(sub + 1) * 1024], LNt[:, :], AF.Identity,
                bias=Rws[:, c0 + 4:c0 + 5], scale=Rws[:, c0 + 3:c0 + 4]),
                reads=[b_LNt, b_Rws], writes=[b_X1])

        for sub in range(4):
            for half in range(2):
                r = claim()
                for f4 in range(4):
                    fcc = half * 4 + f4
                    P.add("pe", lambda e, r=r, f4=f4, fcc=fcc, sub=sub:
                          e.transpose(rot[r][:, f4 * 128:(f4 + 1) * 128],
                                      X1[:, sub * 1024 + fcc * 128:
                                         sub * 1024 + fcc * 128 + 128],
                                      idn[:, :]),
                          reads=[b_X1, b_w["idn"]], writes=[b_rot[r]])
                spacer_t([b_rot[r]])
                P.add("dve", lambda e, r=r, half=half, sub=sub: e.tensor_copy(
                    ap(Q8, half * 4 * 512 + sub * 128,
                       [[4096, 128], [512, 4], [1, 128]]),
                    rot[r][:, :]),
                    reads=[b_rot[r]], writes=[b_Q8])

        for j in range(8):
            r = claim()
            for kc in range(8):
                P.add("pe", lambda e, r=r, j=j, kc=kc:
                      e.matmul(rot[r][:, :],
                               wf8[:, kc * 1024 + j * 128:
                                   kc * 1024 + j * 128 + 128],
                               Q8[:, kc * 512:(kc + 1) * 512],
                               start=(kc == 0), stop=(kc == 7)),
                      reads=[b_w["wf8"], b_Q8], writes=[b_rot[r]])
            spacer([b_rot[r]])
            P.add("dve", lambda e, r=r, j=j: e.tensor_scalar_add(
                Ofm[:, j * 512:(j + 1) * 512], rot[r][:, :],
                biasT[:, j:j + 1]),
                reads=[b_rot[r], b_w["biasT"]], writes=[b_Ofm])

        for sub in range(4):
            for j in range(8):
                P.add("pe", lambda e, j=j, sub=sub: e.matmul(
                    LNt[:, j * 128:(j + 1) * 128],
                    Ofm[:, j * 512 + sub * 128:j * 512 + sub * 128 + 128],
                    idnb[:, :], start=True, stop=True),
                    reads=[b_Ofm, b_w["idnb"]], writes=[b_LNt])
            spacer([b_LNt])
            P.add("dve", lambda e, sub=sub: e.tensor_add(
                X1[:, sub * 1024:(sub + 1) * 1024], LNt[:, :],
                X1[:, sub * 1024:(sub + 1) * 1024]),
                reads=[b_LNt], writes=[b_X1])
            c0 = sub * 8 + 32
            P.add("dve", lambda e, c0=c0, sub=sub: e.reduce_sum(
                Rws[:, c0:c0 + 1], X1[:, sub * 1024:(sub + 1) * 1024],
                axis=AX.X),
                reads=[b_X1], writes=[b_Rws])
            P.add("act", lambda e, c0=c0, sub=sub: e.activation(
                Sq[:, :], X1[:, sub * 1024:(sub + 1) * 1024], AF.Square),
                reads=[b_X1], writes=[b_Sq])
            P.add("dve", lambda e, c0=c0: e.reduce_sum(
                Rws[:, c0 + 1:c0 + 2], Sq[:, :], axis=AX.X),
                reads=[b_Sq], writes=[b_Rws])
            P.add("dve", lambda e, c0=c0: e.tensor_scalar_mul(
                Rws[:, c0 + 2:c0 + 3], Rws[:, c0:c0 + 1], rowsC[:, 3:4]),
                reads=[b_w["rowsC"]], writes=[b_Rws])
            dve_gap()
            P.add("dve", lambda e, c0=c0: e.tensor_mul(
                Rws[:, c0 + 5:c0 + 6], Rws[:, c0 + 2:c0 + 3],
                Rws[:, c0 + 2:c0 + 3]), writes=[b_Rws])
            dve_gap()
            P.add("dve", lambda e, c0=c0: e.scalar_tensor_tensor(
                Rws[:, c0 + 6:c0 + 7], Rws[:, c0 + 1:c0 + 2], rowsC[:, 3:4],
                Rws[:, c0 + 5:c0 + 6], ALU.mult, ALU.subtract),
                reads=[b_w["rowsC"]], writes=[b_Rws])
            P.add("act", lambda e, c0=c0: e.activation(
                Rws[:, c0 + 7:c0 + 8], Rws[:, c0 + 6:c0 + 7], AF.Sqrt,
                bias=rowsC[:, 2:3]),
                reads=[b_w["rowsC"]], writes=[b_Rws])
            P.add("dve", lambda e, c0=c0: e.reciprocal(
                Rws[:, c0 + 3:c0 + 4], Rws[:, c0 + 7:c0 + 8]),
                writes=[b_Rws])
            dve_gap()
            P.add("dve", lambda e, c0=c0: e.scalar_tensor_tensor(
                Rws[:, c0 + 4:c0 + 5], Rws[:, c0 + 2:c0 + 3], rowsC[:, 5:6],
                Rws[:, c0 + 3:c0 + 4], ALU.mult, ALU.mult),
                reads=[b_w["rowsC"]], writes=[b_Rws])
            o = sub % 2
            P.add("act", lambda e, c0=c0, sub=sub, o=o: e.activation(
                OutS[o][:, :], X1[:, sub * 1024:(sub + 1) * 1024],
                AF.Identity, bias=Rws[:, c0 + 4:c0 + 5],
                scale=Rws[:, c0 + 3:c0 + 4]),
                reads=[b_X1, b_Rws], writes=[b_OutS[o]])
            ooff = (s * T + tt * TT + sub * 128) * C
            last_out_op = P.add("sp", lambda e, o=o, ooff=ooff: e.dma_start(
                out=ap(out_d, ooff, [[C, 128], [1, C]]), in_=OutS[o][:, :]),
                reads=[b_OutS[o]], cls="sp_out")
            n_out_dma += 1

    P.add("sp", lambda e: e.nop(), extra=[last_out_op])

    P.finalize()

    with st:
        with (
            nc.semaphore("s_pe") as s_pe,
            nc.semaphore("s_act") as s_act,
            nc.semaphore("s_dve") as s_dve,
            nc.semaphore("s_sp") as s_sp,
            nc.semaphore("s_gp") as s_gp,
            nc.semaphore("s_spp") as s_spp,
            nc.semaphore("s_spx") as s_spx,
            nc.semaphore("s_spo") as s_spo,
            nc.semaphore("s_gpp") as s_gpp,
            nc.semaphore("s_gpx") as s_gpx,
            nc.Block() as block,
        ):
            sems = {"pe": s_pe, "act": s_act, "dve": s_dve,
                    "sp": s_sp, "gp": s_gp,
                    "sp_pre": s_spp, "sp_x": s_spx, "sp_out": s_spo,
                    "gp_pre": s_gpp, "gp_x": s_gpx}

            @block.tensor
            def _(e):
                P.replay("pe", e, sems)

            @block.scalar
            def _(e):
                P.replay("act", e, sems)

            @block.vector
            def _(e):
                P.replay("dve", e, sems)

            @block.sync
            def _(e):
                P.replay("sp", e, sems)

            @block.gpsimd
            def _(e):
                P.replay("gp", e, sems)

    return nc


def kernel(L0_attn_x, L0_attn_Wq, L0_attn_Wk, L0_attn_Wv, L0_attn_Wo,
           L0_ffn_weight, L0_ffn_bias):
    x = np.asarray(L0_attn_x, dtype=np.float32)
    wqT = np.ascontiguousarray(np.asarray(L0_attn_Wq, dtype=np.float32).T)
    wkT = np.ascontiguousarray(np.asarray(L0_attn_Wk, dtype=np.float32).T)
    wvT = np.ascontiguousarray(np.asarray(L0_attn_Wv, dtype=np.float32).T)
    woT = np.ascontiguousarray(np.asarray(L0_attn_Wo, dtype=np.float32).T)
    wfT = np.ascontiguousarray(np.asarray(L0_ffn_weight, dtype=np.float32).T)
    biasT = np.ascontiguousarray(
        np.asarray(L0_ffn_bias, dtype=np.float32).reshape(8, 128).T)
    idn = np.eye(128, dtype=np.float32)
    pq = np.arange(128)[:, None]
    qq = np.arange(128)[None, :]
    m0 = (qq < pq)
    m4 = (qq >= pq)
    masks = np.concatenate([m0, m4], axis=1).astype(ml_dtypes.bfloat16)

    xT = np.ascontiguousarray(x.transpose(0, 2, 1))

    nc = build_program()
    in_maps = []
    for core in range(NCORES):
        shard = np.ascontiguousarray(
            xT[core * BPC:(core + 1) * BPC].reshape(BPC * C, T))
        in_maps.append({
            "xT": shard, "wqT": wqT, "wkT": wkT, "wvT": wvT, "woT": woT,
            "wfT": wfT, "biasT": biasT, "idn": idn, "masks": masks,
        })
    res = run_bass_kernel_spmd(nc, in_maps, core_ids=list(range(NCORES)))
    out = np.concatenate(
        [r["out"].reshape(BPC, T, C) for r in res.results], axis=0)
    return np.ascontiguousarray(out.astype(np.float32))


# version 12
# speedup vs baseline: 1.1229x; 1.1229x over previous
import numpy as np
import ml_dtypes
import concourse.bass as bass
import concourse.mybir as mybir
from concourse.bass_utils import run_bass_kernel_spmd
from contextlib import ExitStack

F32 = mybir.dt.float32
BF16 = mybir.dt.bfloat16
FP8 = mybir.dt.float8e4
AF = mybir.ActivationFunctionType
ALU = mybir.AluOpType
AX = mybir.AxisListType

NCORES = 8
BPC = 4
T = 2048
C = 1024
H = 16
DH = 64
WIN = 512
TT = 512
NTL = T // TT
QT = 128
NQT = TT // QT

ENGS = ("pe", "act", "dve", "sp", "gp")


class Op:
    __slots__ = ("eng", "cls", "fn", "deps", "idx", "rank", "marked")

    def __init__(self, eng, fn, deps, idx, cls):
        self.eng = eng
        self.cls = cls
        self.fn = fn
        self.deps = deps
        self.idx = idx
        self.rank = 0
        self.marked = False


class Buf:
    __slots__ = ("w", "r")

    def __init__(self):
        self.w = None
        self.r = []


class Prog:
    def __init__(self):
        self.ops = {e: [] for e in ENGS}
        self.prelude_last = {}
        self.scratch = None

    def mark_prelude(self):
        for e in ("sp", "gp"):
            for op in self.ops[e]:
                self.prelude_last[op.cls] = op

    def add(self, eng, fn, reads=(), writes=(), extra=(), cls=None):
        deps = []
        for b in reads:
            if b.w is not None:
                deps.append(b.w)
        for b in writes:
            if b.w is not None:
                deps.append(b.w)
            deps.extend(b.r)
        deps.extend(extra)
        op = Op(eng, fn, deps, len(self.ops[eng]), cls or eng)
        self.ops[eng].append(op)
        for b in reads:
            b.r.append(op)
        for b in writes:
            b.w = op
            b.r = []
        return op

    def finalize(self):
        for e in ("sp", "gp"):
            for op in self.ops[e]:
                op.marked = True
        for e in ENGS:
            for op in self.ops[e]:
                for d in op.deps:
                    if d.eng != op.eng:
                        d.marked = True
        ranks = {}
        for e in ENGS:
            for op in self.ops[e]:
                if op.marked:
                    ranks[op.cls] = ranks.get(op.cls, 0) + 1
                    op.rank = ranks[op.cls]

    def replay(self, eng_name, eng, sems):
        maxw = {}
        for op in self.ops[eng_name]:
            for d in op.deps:
                if d.eng == eng_name and d.eng not in ("sp", "gp"):
                    continue
                if d.cls in self.prelude_last:
                    last = self.prelude_last[d.cls]
                    if d.idx <= last.idx:
                        d = last
                v = d.rank * (16 if d.eng in ("sp", "gp") else 1)
                if v > maxw.get(d.cls, 0):
                    eng.wait_ge(sems[d.cls], v)
                    maxw[d.cls] = v
            inst = op.fn(eng)
            if op.marked:
                if op.eng == "dve":
                    inst = eng.tensor_copy(self.scratch[0:1, 8:9],
                                           self.scratch[0:1, 9:10])
                elif op.eng == "act":
                    inst = eng.copy(self.scratch[0:1, 0:1],
                                    self.scratch[0:1, 1:2])
                inst.then_inc(sems[op.cls], 16 if op.eng in ("sp", "gp") else 1)


def band_chunks(tt, qt):
    q0 = tt * TT + qt * QT
    start = max(0, q0 - WIN)
    return list(range(start, q0 + QT, QT)), q0, start


def build_program():
    nc = bass.Bass("TRN2", target_bir_lowering=False)
    ap = bass.AP

    xT = nc.dram_tensor("xT", [BPC * C, T], F32, kind="ExternalInput")
    wq_d = nc.dram_tensor("wqT", [C, C], F32, kind="ExternalInput")
    wk_d = nc.dram_tensor("wkT", [C, C], F32, kind="ExternalInput")
    wv_d = nc.dram_tensor("wvT", [C, C], F32, kind="ExternalInput")
    wo_d = nc.dram_tensor("woT", [C, C], F32, kind="ExternalInput")
    wf_d = nc.dram_tensor("wfT", [C, C], F32, kind="ExternalInput")
    bias_d = nc.dram_tensor("biasT", [128, 8], F32, kind="ExternalInput")
    idn_d = nc.dram_tensor("idn", [128, 128], F32, kind="ExternalInput")
    msk_d = nc.dram_tensor("masks", [128, 256], BF16, kind="ExternalInput")
    out_d = nc.dram_tensor("out", [BPC * T, C], F32, kind="ExternalOutput")

    st = ExitStack()
    sb = lambda name, shape, dt: st.enter_context(nc.sbuf_tensor(name, shape, dt))
    psum = lambda name, shape, dt: st.enter_context(nc.psum_tensor(name, shape, dt))

    wq = sb("wq", [128, 8192], BF16)
    wk = sb("wk", [128, 8192], BF16)
    wv = sb("wv", [128, 8192], BF16)
    wo = sb("wo", [128, 8192], BF16)
    wf8 = sb("wf8", [128, 8192], FP8)
    biasT = sb("biasTs", [128, 8], F32)
    idn = sb("idns", [128, 128], F32)
    idnb = sb("idnb", [128, 128], BF16)
    masks = sb("maskss", [128, 256], BF16)
    rowsC = sb("rowsC", [128, 8], F32)
    x32 = [sb("x32a", [128, 4096], F32), sb("x32b", [128, 4096], F32)]
    xb = sb("xb", [128, 4096], BF16)
    Qb = sb("Qb", [128, 8192], BF16)
    Kr = [sb("k0", [128, 4096], BF16), sb("k1", [128, 4096], BF16)]
    Vr = [sb("v0", [128, 4160], BF16), sb("v1", [128, 4160], BF16)]
    Pt = [sb("p0", [128, 640], BF16), sb("p1", [128, 640], BF16),
          sb("p2", [128, 640], BF16), sb("p3", [128, 640], BF16)]
    Ofm = sb("ofm", [128, 4096], BF16)
    Otm = xb
    X1 = sb("x1tm", [128, 4096], F32)
    Q8 = sb("q8", [128, 4096], FP8)
    OutS = [sb("os0", [128, 1024], F32), sb("os1", [128, 1024], F32)]
    Sq = sb("sq", [128, 1024], BF16)
    Rws = sb("rws", [128, 64], F32)
    Scrp = sb("scrp", [128, 16], F32)
    Rinv = sb("rinv", [128, 64], F32)

    rot = [psum("rot0", [128, 512], F32), psum("rot1", [128, 512], F32)]
    S0 = psum("s0", [128, 1024], F32)
    S1 = psum("s1", [128, 1024], F32)
    LNt = psum("lnt", [128, 1024], F32)

    P = Prog()
    b_w = {n: Buf() for n in ("wq", "wk", "wv", "wo", "wf8", "biasT", "idn",
                              "idnb", "masks", "rowsC")}
    b_x32 = [Buf(), Buf()]
    b_xb = Buf()
    b_Otm = b_xb
    b_Q = Buf()
    b_K = [Buf(), Buf()]
    b_V = [Buf(), Buf()]
    b_Pt = [Buf(), Buf(), Buf(), Buf()]
    b_Ofm = Buf()
    b_X1 = Buf()
    b_Q8 = Buf()
    b_OutS = [Buf(), Buf()]
    b_Sq = Buf()
    b_Rws = Buf()
    b_Rinv = Buf()
    b_rot = [Buf(), Buf()]
    b_S = [Buf(), Buf()]
    b_LNt = Buf()

    WSB = {"q": wq, "k": wk, "v": wv, "o": wo}

    for nm, dram in (("wq", wq_d), ("wk", wk_d), ("wv", wv_d), ("wo", wo_d)):
        dst = WSB[nm[1]]
        P.add("gp", lambda e, dst=dst, dram=dram: e.dma_start(
            out=dst[:, :],
            in_=ap(dram, 0, [[C, 128], [128 * C, 8], [1, C]])),
            writes=[b_w[nm]], cls="gp_pre")
    P.add("sp", lambda e: e.dma_start(out=biasT[:, :],
          in_=ap(bias_d, 0, [[8, 128], [1, 8]])), writes=[b_w["biasT"]],
          cls="sp_pre")
    P.add("sp", lambda e: e.dma_start(out=idn[:, :],
          in_=ap(idn_d, 0, [[128, 128], [1, 128]])), writes=[b_w["idn"]],
          cls="sp_pre")
    P.add("sp", lambda e: e.dma_start(out=masks[:, :],
          in_=ap(msk_d, 0, [[256, 128], [1, 256]])), writes=[b_w["masks"]],
          cls="sp_pre")
    P.add("dve", lambda e: e.tensor_copy(idnb[:, :], idn[:, :]),
          reads=[b_w["idn"]], writes=[b_w["idnb"]])
    for col, val in ((0, 448.0), (1, -448.0), (2, 1e-5), (3, 1.0 / 1024.0),
                     (4, -12.0), (5, -1.0)):
        P.add("dve", lambda e, col=col, val=val:
              e.memset(rowsC[:, col:col + 1], val), writes=[b_w["rowsC"]])
    P.add("dve", lambda e: e.memset(Qb[:, :], 0.0), writes=[b_Q])
    for fn in (AF.Exp, AF.Square, AF.Sqrt, AF.Identity):
        P.add("act", lambda e, fn=fn: e.activation(
            Scrp[:, 0:8], rowsC[:, 0:8], fn),
            reads=[b_w["rowsC"]])
    P.add("act", lambda e: e.copy(Scrp[:, 0:8], rowsC[:, 0:8]),
          reads=[b_w["rowsC"]])
    for sl in range(2):
        P.add("dve", lambda e, sl=sl: e.memset(
            ap(Vr[sl], 64, [[4160, 128], [1040, 4], [65, 16]]), 1.0),
            writes=[b_V[sl]])
    for half in range(2):
        P.add("sp", lambda e, half=half: e.dma_start(
            out=x32[half][:, :],
            in_=ap(wf_d, half * 512 * C, [[C, 128], [128 * C, 4], [1, C]])),
            writes=[b_x32[half]], cls="sp_pre")
        for kc in range(4):
            kk = half * 4 + kc
            P.add("dve", lambda e, half=half, kc=kc, kk=kk: e.tensor_scalar(
                wf8[:, kk * 1024:(kk + 1) * 1024],
                x32[half][:, kc * 1024:(kc + 1) * 1024],
                rowsC[:, 0:1], rowsC[:, 1:2], ALU.min, ALU.max),
                reads=[b_x32[half], b_w["rowsC"]], writes=[b_w["wf8"]])

    P.scratch = Scrp
    P.mark_prelude()

    n_out_dma = 0
    last_out_op = None
    for tile in range(BPC * NTL):
        s, tt = divmod(tile, NTL)
        p = tile % 2

        xoff = s * C * T + tt * TT
        P.add("sp", lambda e, xoff=xoff, p=p: e.dma_start(
            out=x32[p][:, :],
            in_=ap(xT, xoff, [[T, 128], [128 * T, 8], [1, TT]])),
            writes=[b_x32[p]], cls="sp_x")
        P.add("gp", lambda e, xoff=xoff: e.dma_start(
            out=xb[:, :],
            in_=ap(xT, xoff, [[T, 128], [128 * T, 8], [1, TT]])),
            writes=[b_xb], cls="gp_x")

        rot_i = [0]

        def claim():
            r = rot_i[0] & 1
            rot_i[0] += 1
            return r

        def dve_gap(n=3):
            for _ in range(n):
                P.add("dve", lambda e: e.tensor_copy(
                    Scrp[0:1, 10:11], Scrp[0:1, 11:12]))

        def spacer_t(bufs):
            P.add("pe", lambda e: e.transpose(
                S1[0:1, 768:896], idn[:, 0:1], idn[:, :]),
                reads=[b_w["idn"]], writes=[b_S[1]])
            spacer(bufs)

        def spacer(bufs):
            w = list(bufs)
            if b_S[1] not in w:
                w.append(b_S[1])
            P.add("pe", lambda e: e.matmul(
                S1[0:1, 1023:1024], idnb[:, 0:1], idnb[:, 0:1],
                start=True, stop=True), writes=w)

        for nm, dstbuf, dst in (("q", b_Q, Qb), ("k", b_K[tt % 2], Kr[tt % 2])):
            for j in range(8):
                r = claim()
                for kc in range(8):
                    P.add("pe", lambda e, r=r, nm=nm, j=j, kc=kc:
                          e.matmul(rot[r][:, :],
                                   WSB[nm][:, kc * 1024 + j * 128:
                                           kc * 1024 + j * 128 + 128],
                                   xb[:, kc * 512:(kc + 1) * 512],
                                   start=(kc == 0), stop=(kc == 7)),
                          reads=[b_w["w" + nm], b_xb], writes=[b_rot[r]])
                spacer([b_rot[r]])
                if nm == "k":
                    P.add("dve", lambda e, r=r, dst=dst, j=j: e.tensor_copy(
                        dst[:, j * 512:(j + 1) * 512], rot[r][:, :]),
                        reads=[b_rot[r]], writes=[dstbuf])
                else:
                    P.add("dve", lambda e, r=r, dst=dst, j=j: e.tensor_copy(
                        dst[0:64, (2 * j) * 512:(2 * j + 1) * 512],
                        rot[r][0:64, :]),
                        reads=[b_rot[r]], writes=[dstbuf])
                    P.add("dve", lambda e, r=r, dst=dst, j=j: e.tensor_copy(
                        dst[64:128, (2 * j + 1) * 512:(2 * j + 2) * 512],
                        rot[r][64:128, :]),
                        reads=[b_rot[r]], writes=[dstbuf])
        vsl = tt % 2
        for ts in range(4):
            for fh in range(2):
                r = claim()
                for kc in range(8):
                    P.add("pe", lambda e, r=r, ts=ts, fh=fh, kc=kc:
                          e.matmul(rot[r][:, :],
                                   xb[:, kc * 512 + ts * 128:
                                      kc * 512 + ts * 128 + 128],
                                   wv[:, kc * 1024 + fh * 512:
                                      kc * 1024 + fh * 512 + 512],
                                   start=(kc == 0), stop=(kc == 7)),
                          reads=[b_w["wv"], b_xb], writes=[b_rot[r]])
                spacer([b_rot[r]])
                P.add("dve", lambda e, r=r, vsl=vsl, ts=ts, fh=fh:
                      e.tensor_copy(
                          ap(Vr[vsl], ts * 1040 + fh * 520,
                             [[4160, 128], [65, 8], [1, 64]]),
                          rot[r][:, :]),
                      reads=[b_rot[r]], writes=[b_V[vsl]])

        def emit_S(h, qts):
            binfo = {qt: band_chunks(tt, qt) for qt in qts}
            union = sorted({k for qt in qts for k in binfo[qt][0]})
            fc = h // 2
            for kabs in union:
                ksl = (kabs // TT) % 2
                kcol = kabs % TT
                for qt in qts:
                    chunks, q0, bstart = binfo[qt]
                    if kabs not in chunks:
                        continue
                    crel = (kabs - bstart) // QT
                    sreg = S0 if qt % 2 == 0 else S1
                    P.add("pe", lambda e, ksl=ksl, kcol=kcol, h=h,
                          fc=fc, qt=qt, crel=crel, sreg=sreg:
                          e.matmul(sreg[:, crel * 128:(crel + 1) * 128],
                                   Kr[ksl][:, fc * 512 + kcol:
                                           fc * 512 + kcol + 128],
                                   Qb[:, h * 512 + qt * 128:
                                      h * 512 + qt * 128 + 128],
                                   start=True, stop=True),
                          reads=[b_K[ksl], b_Q],
                          writes=[b_S[qt % 2]])
            spacer([b_S[0], b_S[1]])
            for qt in qts:
                chunks, q0, bstart = binfo[qt]
                n = len(chunks)
                sreg = S0 if qt % 2 == 0 else S1
                P.add("act", lambda e, sreg=sreg, n=n, qt=qt:
                      e.activation(Pt[qt][:, 0:n * 128],
                                   sreg[:, 0:n * 128], AF.Exp,
                                   bias=rowsC[:, 4:5], scale=0.125),
                      reads=[b_S[qt % 2], b_w["rowsC"]],
                      writes=[b_Pt[qt]])
                if q0 >= WIN:
                    P.add("dve", lambda e, qt=qt, n=n: e.tensor_tensor(
                        ap(Pt[qt], 0, [[640, 128], [(n - 1) * 128, 2],
                                       [1, 128]]),
                        ap(Pt[qt], 0, [[640, 128], [(n - 1) * 128, 2],
                                       [1, 128]]),
                        ap(masks, 0, [[256, 128], [128, 2], [1, 128]]),
                        ALU.mult),
                        reads=[b_w["masks"]], writes=[b_Pt[qt]])
                else:
                    P.add("dve", lambda e, qt=qt, n=n: e.tensor_tensor(
                        Pt[qt][:, (n - 1) * 128:n * 128],
                        Pt[qt][:, (n - 1) * 128:n * 128],
                        masks[:, 128:256], ALU.mult),
                        reads=[b_w["masks"]], writes=[b_Pt[qt]])

        def emit_PV(h, qts):
            pr = h % 2
            for qt in qts:
                chunks, q0, bstart = band_chunks(tt, qt)
                n = len(chunks)
                for ci, kabs in enumerate(chunks):
                    ksl = (kabs // TT) % 2
                    ts = (kabs % TT) // 128
                    P.add("pe", lambda e, qt=qt, ci=ci, n=n, ksl=ksl,
                          ts=ts, h=h, pr=pr:
                          e.matmul(rot[pr][:, qt * 65:qt * 65 + 65],
                                   Pt[qt][:, ci * 128:(ci + 1) * 128],
                                   Vr[ksl][:, ts * 1040 + h * 65:
                                           ts * 1040 + h * 65 + 65],
                                   start=(ci == 0), stop=(ci == n - 1)),
                          reads=[b_Pt[qt], b_V[ksl]],
                          writes=[b_rot[pr]])

        def emit_Orecip(h):
            pr = h % 2
            spacer([b_rot[pr]])
            P.add("dve", lambda e, pr=pr, h=h: e.reciprocal(
                Rinv[:, h * 4:h * 4 + 4],
                ap(rot[pr], 64, [[512, 128], [65, 4]])),
                reads=[b_rot[pr]], writes=[b_Rinv])

        def emit_Oevac(h):
            pr = h % 2
            for qt in range(4):
                P.add("dve", lambda e, pr=pr, h=h, qt=qt: e.tensor_scalar_mul(
                    ap(Otm, qt * 1024 + h * 64, [[4096, 128], [1, 64]]),
                    ap(rot[pr], qt * 65, [[512, 128], [1, 64]]),
                    Rinv[:, h * 4 + qt:h * 4 + qt + 1]),
                    reads=[b_rot[pr], b_Rinv], writes=[b_Otm])

        emit_S(0, [0, 1])
        emit_S(0, [2, 3])
        emit_PV(0, [0, 1])
        for h in range(1, 16):
            emit_S(h, [0, 1])
            emit_PV(h - 1, [2, 3])
            emit_Orecip(h - 1)
            if h >= 2:
                emit_Oevac(h - 2)
            emit_S(h, [2, 3])
            emit_PV(h, [0, 1])
        emit_PV(15, [2, 3])
        emit_Orecip(15)
        dve_gap()
        emit_Oevac(14)
        emit_Oevac(15)

        for qt in range(4):
            for half in range(2):
                r = claim()
                for f4 in range(4):
                    fcc = half * 4 + f4
                    P.add("pe", lambda e, r=r, qt=qt, fcc=fcc, f4=f4:
                          e.matmul(rot[r][:, f4 * 128:(f4 + 1) * 128],
                                   Otm[:, qt * 1024 + fcc * 128:
                                       qt * 1024 + fcc * 128 + 128],
                                   idnb[:, :], start=True, stop=True),
                          reads=[b_Otm, b_w["idnb"]], writes=[b_rot[r]])
                spacer([b_rot[r]])
                P.add("dve", lambda e, r=r, qt=qt, half=half: e.tensor_copy(
                    ap(Ofm, half * 4 * 512 + qt * 128,
                       [[4096, 128], [512, 4], [1, 128]]),
                    rot[r][:, :]),
                    reads=[b_rot[r]], writes=[b_Ofm])

        for j in range(8):
            r = claim()
            for kc in range(8):
                P.add("pe", lambda e, r=r, j=j, kc=kc:
                      e.matmul(rot[r][:, :],
                               wo[:, kc * 1024 + j * 128:
                                  kc * 1024 + j * 128 + 128],
                               Ofm[:, kc * 512:(kc + 1) * 512],
                               start=(kc == 0), stop=(kc == 7)),
                      reads=[b_w["wo"], b_Ofm], writes=[b_rot[r]])
            spacer([b_rot[r]])
            P.add("dve", lambda e, r=r, j=j, p=p: e.tensor_add(
                x32[p][:, j * 512:(j + 1) * 512], rot[r][:, :],
                x32[p][:, j * 512:(j + 1) * 512]),
                reads=[b_rot[r]], writes=[b_x32[p]])

        for sub in range(4):
            for fcc in range(8):
                P.add("pe", lambda e, fcc=fcc, sub=sub, p=p: e.transpose(
                    LNt[:, fcc * 128:(fcc + 1) * 128],
                    x32[p][:, fcc * 512 + sub * 128:
                           fcc * 512 + sub * 128 + 128],
                    idn[:, :]),
                    reads=[b_x32[p], b_w["idn"]], writes=[b_LNt])
            spacer_t([b_LNt])
            c0 = sub * 8
            red_op = P.add("dve", lambda e, c0=c0: e.reduce_sum(
                Rws[:, c0:c0 + 1], LNt[:, :], axis=AX.X),
                reads=[b_LNt], writes=[b_Rws])
            P.add("act", lambda e, c0=c0: e.activation(
                Sq[:, :], LNt[:, :], AF.Square),
                reads=[b_LNt], writes=[b_Sq], extra=[red_op])
            P.add("dve", lambda e, c0=c0: e.reduce_sum(
                Rws[:, c0 + 1:c0 + 2], Sq[:, :], axis=AX.X),
                reads=[b_Sq], writes=[b_Rws])
            P.add("dve", lambda e, c0=c0: e.tensor_scalar_mul(
                Rws[:, c0 + 2:c0 + 3], Rws[:, c0:c0 + 1], rowsC[:, 3:4]),
                reads=[b_w["rowsC"]], writes=[b_Rws])
            dve_gap()
            P.add("dve", lambda e, c0=c0: e.tensor_mul(
                Rws[:, c0 + 5:c0 + 6], Rws[:, c0 + 2:c0 + 3],
                Rws[:, c0 + 2:c0 + 3]), writes=[b_Rws])
            dve_gap()
            P.add("dve", lambda e, c0=c0: e.scalar_tensor_tensor(
                Rws[:, c0 + 6:c0 + 7], Rws[:, c0 + 1:c0 + 2], rowsC[:, 3:4],
                Rws[:, c0 + 5:c0 + 6], ALU.mult, ALU.subtract),
                reads=[b_w["rowsC"]], writes=[b_Rws])
            P.add("act", lambda e, c0=c0: e.activation(
                Rws[:, c0 + 7:c0 + 8], Rws[:, c0 + 6:c0 + 7], AF.Sqrt,
                bias=rowsC[:, 2:3]),
                reads=[b_w["rowsC"]], writes=[b_Rws])
            P.add("dve", lambda e, c0=c0: e.reciprocal(
                Rws[:, c0 + 3:c0 + 4], Rws[:, c0 + 7:c0 + 8]),
                writes=[b_Rws])
            dve_gap()
            P.add("dve", lambda e, c0=c0: e.scalar_tensor_tensor(
                Rws[:, c0 + 4:c0 + 5], Rws[:, c0 + 2:c0 + 3], rowsC[:, 5:6],
                Rws[:, c0 + 3:c0 + 4], ALU.mult, ALU.mult),
                reads=[b_w["rowsC"]], writes=[b_Rws])
            P.add("act", lambda e, c0=c0, sub=sub: e.activation(
                X1[:, sub * 1024:(sub + 1) * 1024], LNt[:, :], AF.Identity,
                bias=Rws[:, c0 + 4:c0 + 5], scale=Rws[:, c0 + 3:c0 + 4]),
                reads=[b_LNt, b_Rws], writes=[b_X1])

        for sub in range(4):
            for half in range(2):
                r = claim()
                for f4 in range(4):
                    fcc = half * 4 + f4
                    P.add("pe", lambda e, r=r, f4=f4, fcc=fcc, sub=sub:
                          e.transpose(rot[r][:, f4 * 128:(f4 + 1) * 128],
                                      X1[:, sub * 1024 + fcc * 128:
                                         sub * 1024 + fcc * 128 + 128],
                                      idn[:, :]),
                          reads=[b_X1, b_w["idn"]], writes=[b_rot[r]])
                spacer_t([b_rot[r]])
                P.add("dve", lambda e, r=r, half=half, sub=sub: e.tensor_copy(
                    ap(Q8, half * 4 * 512 + sub * 128,
                       [[4096, 128], [512, 4], [1, 128]]),
                    rot[r][:, :]),
                    reads=[b_rot[r]], writes=[b_Q8])

        for j in range(8):
            r = claim()
            for kc in range(8):
                P.add("pe", lambda e, r=r, j=j, kc=kc:
                      e.matmul(rot[r][:, :],
                               wf8[:, kc * 1024 + j * 128:
                                   kc * 1024 + j * 128 + 128],
                               Q8[:, kc * 512:(kc + 1) * 512],
                               start=(kc == 0), stop=(kc == 7)),
                      reads=[b_w["wf8"], b_Q8], writes=[b_rot[r]])
            spacer([b_rot[r]])
            P.add("dve", lambda e, r=r, j=j: e.tensor_scalar_add(
                Ofm[:, j * 512:(j + 1) * 512], rot[r][:, :],
                biasT[:, j:j + 1]),
                reads=[b_rot[r], b_w["biasT"]], writes=[b_Ofm])

        for sub in range(4):
            for j in range(8):
                P.add("pe", lambda e, j=j, sub=sub: e.matmul(
                    LNt[:, j * 128:(j + 1) * 128],
                    Ofm[:, j * 512 + sub * 128:j * 512 + sub * 128 + 128],
                    idnb[:, :], start=True, stop=True),
                    reads=[b_Ofm, b_w["idnb"]], writes=[b_LNt])
            spacer([b_LNt])
            P.add("dve", lambda e, sub=sub: e.tensor_add(
                X1[:, sub * 1024:(sub + 1) * 1024], LNt[:, :],
                X1[:, sub * 1024:(sub + 1) * 1024]),
                reads=[b_LNt], writes=[b_X1])
            c0 = sub * 8 + 32
            P.add("dve", lambda e, c0=c0, sub=sub: e.reduce_sum(
                Rws[:, c0:c0 + 1], X1[:, sub * 1024:(sub + 1) * 1024],
                axis=AX.X),
                reads=[b_X1], writes=[b_Rws])
            P.add("act", lambda e, c0=c0, sub=sub: e.activation(
                Sq[:, :], X1[:, sub * 1024:(sub + 1) * 1024], AF.Square),
                reads=[b_X1], writes=[b_Sq])
            P.add("dve", lambda e, c0=c0: e.reduce_sum(
                Rws[:, c0 + 1:c0 + 2], Sq[:, :], axis=AX.X),
                reads=[b_Sq], writes=[b_Rws])
            P.add("dve", lambda e, c0=c0: e.tensor_scalar_mul(
                Rws[:, c0 + 2:c0 + 3], Rws[:, c0:c0 + 1], rowsC[:, 3:4]),
                reads=[b_w["rowsC"]], writes=[b_Rws])
            dve_gap()
            P.add("dve", lambda e, c0=c0: e.tensor_mul(
                Rws[:, c0 + 5:c0 + 6], Rws[:, c0 + 2:c0 + 3],
                Rws[:, c0 + 2:c0 + 3]), writes=[b_Rws])
            dve_gap()
            P.add("dve", lambda e, c0=c0: e.scalar_tensor_tensor(
                Rws[:, c0 + 6:c0 + 7], Rws[:, c0 + 1:c0 + 2], rowsC[:, 3:4],
                Rws[:, c0 + 5:c0 + 6], ALU.mult, ALU.subtract),
                reads=[b_w["rowsC"]], writes=[b_Rws])
            P.add("act", lambda e, c0=c0: e.activation(
                Rws[:, c0 + 7:c0 + 8], Rws[:, c0 + 6:c0 + 7], AF.Sqrt,
                bias=rowsC[:, 2:3]),
                reads=[b_w["rowsC"]], writes=[b_Rws])
            P.add("dve", lambda e, c0=c0: e.reciprocal(
                Rws[:, c0 + 3:c0 + 4], Rws[:, c0 + 7:c0 + 8]),
                writes=[b_Rws])
            dve_gap()
            P.add("dve", lambda e, c0=c0: e.scalar_tensor_tensor(
                Rws[:, c0 + 4:c0 + 5], Rws[:, c0 + 2:c0 + 3], rowsC[:, 5:6],
                Rws[:, c0 + 3:c0 + 4], ALU.mult, ALU.mult),
                reads=[b_w["rowsC"]], writes=[b_Rws])
            o = sub % 2
            P.add("act", lambda e, c0=c0, sub=sub, o=o: e.activation(
                OutS[o][:, :], X1[:, sub * 1024:(sub + 1) * 1024],
                AF.Identity, bias=Rws[:, c0 + 4:c0 + 5],
                scale=Rws[:, c0 + 3:c0 + 4]),
                reads=[b_X1, b_Rws], writes=[b_OutS[o]])
            ooff = (s * T + tt * TT + sub * 128) * C
            last_out_op = P.add("sp", lambda e, o=o, ooff=ooff: e.dma_start(
                out=ap(out_d, ooff, [[C, 128], [1, C]]), in_=OutS[o][:, :]),
                reads=[b_OutS[o]], cls="sp_out")
            n_out_dma += 1

    P.add("sp", lambda e: e.nop(), extra=[last_out_op])

    P.finalize()

    with st:
        with (
            nc.semaphore("s_pe") as s_pe,
            nc.semaphore("s_act") as s_act,
            nc.semaphore("s_dve") as s_dve,
            nc.semaphore("s_sp") as s_sp,
            nc.semaphore("s_gp") as s_gp,
            nc.semaphore("s_spp") as s_spp,
            nc.semaphore("s_spx") as s_spx,
            nc.semaphore("s_spo") as s_spo,
            nc.semaphore("s_gpp") as s_gpp,
            nc.semaphore("s_gpx") as s_gpx,
            nc.Block() as block,
        ):
            sems = {"pe": s_pe, "act": s_act, "dve": s_dve,
                    "sp": s_sp, "gp": s_gp,
                    "sp_pre": s_spp, "sp_x": s_spx, "sp_out": s_spo,
                    "gp_pre": s_gpp, "gp_x": s_gpx}

            @block.tensor
            def _(e):
                P.replay("pe", e, sems)

            @block.scalar
            def _(e):
                P.replay("act", e, sems)

            @block.vector
            def _(e):
                P.replay("dve", e, sems)

            @block.sync
            def _(e):
                P.replay("sp", e, sems)

            @block.gpsimd
            def _(e):
                P.replay("gp", e, sems)

    return nc


def kernel(L0_attn_x, L0_attn_Wq, L0_attn_Wk, L0_attn_Wv, L0_attn_Wo,
           L0_ffn_weight, L0_ffn_bias):
    x = np.asarray(L0_attn_x, dtype=np.float32)
    wqT = np.ascontiguousarray(np.asarray(L0_attn_Wq, dtype=np.float32).T)
    wkT = np.ascontiguousarray(np.asarray(L0_attn_Wk, dtype=np.float32).T)
    wvT = np.ascontiguousarray(np.asarray(L0_attn_Wv, dtype=np.float32).T)
    woT = np.ascontiguousarray(np.asarray(L0_attn_Wo, dtype=np.float32).T)
    wfT = np.ascontiguousarray(np.asarray(L0_ffn_weight, dtype=np.float32).T)
    biasT = np.ascontiguousarray(
        np.asarray(L0_ffn_bias, dtype=np.float32).reshape(8, 128).T)
    idn = np.eye(128, dtype=np.float32)
    pq = np.arange(128)[:, None]
    qq = np.arange(128)[None, :]
    m0 = (qq < pq)
    m4 = (qq >= pq)
    masks = np.concatenate([m0, m4], axis=1).astype(ml_dtypes.bfloat16)

    xT = np.ascontiguousarray(x.transpose(0, 2, 1))

    nc = build_program()
    in_maps = []
    for core in range(NCORES):
        shard = np.ascontiguousarray(
            xT[core * BPC:(core + 1) * BPC].reshape(BPC * C, T))
        in_maps.append({
            "xT": shard, "wqT": wqT, "wkT": wkT, "wvT": wvT, "woT": woT,
            "wfT": wfT, "biasT": biasT, "idn": idn, "masks": masks,
        })
    res = run_bass_kernel_spmd(nc, in_maps, core_ids=list(range(NCORES)))
    out = np.concatenate(
        [r["out"].reshape(BPC, T, C) for r in res.results], axis=0)
    return np.ascontiguousarray(out.astype(np.float32))


# version 13 (final)
# speedup vs baseline: 1.2419x; 1.1060x over previous
import numpy as np
import ml_dtypes
import concourse.bass as bass
import concourse.mybir as mybir
from concourse.bass_utils import run_bass_kernel_spmd
from contextlib import ExitStack

F32 = mybir.dt.float32
BF16 = mybir.dt.bfloat16
FP8 = mybir.dt.float8e4
AF = mybir.ActivationFunctionType
ALU = mybir.AluOpType
AX = mybir.AxisListType

NCORES = 8
BPC = 4
T = 2048
C = 1024
H = 16
DH = 64
WIN = 512
TT = 512
NTL = T // TT
QT = 128
NQT = TT // QT

ENGS = ("pe", "act", "dve", "sp", "gp")


class Op:
    __slots__ = ("eng", "cls", "fn", "deps", "idx", "rank", "marked")

    def __init__(self, eng, fn, deps, idx, cls):
        self.eng = eng
        self.cls = cls
        self.fn = fn
        self.deps = deps
        self.idx = idx
        self.rank = 0
        self.marked = False


class Buf:
    __slots__ = ("w", "r")

    def __init__(self):
        self.w = None
        self.r = []


class Prog:
    def __init__(self):
        self.ops = {e: [] for e in ENGS}
        self.prelude_last = {}
        self.scratch = None

    def mark_prelude(self):
        for e in ("sp", "gp"):
            for op in self.ops[e]:
                self.prelude_last[op.cls] = op

    def add(self, eng, fn, reads=(), writes=(), extra=(), cls=None):
        deps = []
        for b in reads:
            if b.w is not None:
                deps.append(b.w)
        for b in writes:
            if b.w is not None:
                deps.append(b.w)
            deps.extend(b.r)
        deps.extend(extra)
        op = Op(eng, fn, deps, len(self.ops[eng]), cls or eng)
        self.ops[eng].append(op)
        for b in reads:
            b.r.append(op)
        for b in writes:
            b.w = op
            b.r = []
        return op

    def finalize(self):
        for e in ("sp", "gp"):
            for op in self.ops[e]:
                op.marked = True
        for e in ENGS:
            for op in self.ops[e]:
                for d in op.deps:
                    if d.eng != op.eng:
                        d.marked = True
        ranks = {}
        for e in ENGS:
            for op in self.ops[e]:
                if op.marked:
                    ranks[op.cls] = ranks.get(op.cls, 0) + 1
                    op.rank = ranks[op.cls]

    def replay(self, eng_name, eng, sems):
        maxw = {}
        for op in self.ops[eng_name]:
            for d in op.deps:
                if d.eng == eng_name and d.eng not in ("sp", "gp"):
                    continue
                if d.cls in self.prelude_last:
                    last = self.prelude_last[d.cls]
                    if d.idx <= last.idx:
                        d = last
                v = d.rank * (16 if d.eng in ("sp", "gp") else 1)
                if v > maxw.get(d.cls, 0):
                    eng.wait_ge(sems[d.cls], v)
                    maxw[d.cls] = v
            inst = op.fn(eng)
            if op.marked:
                if op.eng == "dve":
                    inst = eng.tensor_copy(self.scratch[0:1, 8:9],
                                           self.scratch[0:1, 9:10])
                elif op.eng == "act":
                    inst = eng.copy(self.scratch[0:1, 0:1],
                                    self.scratch[0:1, 1:2])
                inst.then_inc(sems[op.cls], 16 if op.eng in ("sp", "gp") else 1)


def band_chunks(tt, qt):
    q0 = tt * TT + qt * QT
    start = max(0, q0 - WIN)
    return list(range(start, q0 + QT, QT)), q0, start


def build_program():
    nc = bass.Bass("TRN2", target_bir_lowering=False)
    ap = bass.AP

    xT = nc.dram_tensor("xT", [BPC * C, T], F32, kind="ExternalInput")
    wq_d = nc.dram_tensor("wqT", [C, C], F32, kind="ExternalInput")
    wk_d = nc.dram_tensor("wkT", [C, C], F32, kind="ExternalInput")
    wv_d = nc.dram_tensor("wvT", [C, C], F32, kind="ExternalInput")
    wo_d = nc.dram_tensor("woT", [C, C], F32, kind="ExternalInput")
    wf_d = nc.dram_tensor("wfT", [C, C], F32, kind="ExternalInput")
    bias_d = nc.dram_tensor("biasT", [128, 8], F32, kind="ExternalInput")
    idn_d = nc.dram_tensor("idn", [128, 128], F32, kind="ExternalInput")
    msk_d = nc.dram_tensor("masks", [128, 256], BF16, kind="ExternalInput")
    out_d = nc.dram_tensor("out", [BPC * T, C], F32, kind="ExternalOutput")

    st = ExitStack()
    sb = lambda name, shape, dt: st.enter_context(nc.sbuf_tensor(name, shape, dt))
    psum = lambda name, shape, dt: st.enter_context(nc.psum_tensor(name, shape, dt))

    wq = sb("wq", [128, 8192], BF16)
    wk = sb("wk", [128, 8192], BF16)
    wv = sb("wv", [128, 8192], BF16)
    wo = sb("wo", [128, 8192], BF16)
    wf8 = sb("wf8", [128, 8192], FP8)
    biasT = sb("biasTs", [128, 8], F32)
    idn = sb("idns", [128, 128], F32)
    idnb = sb("idnb", [128, 128], BF16)
    masks = sb("maskss", [128, 256], BF16)
    rowsC = sb("rowsC", [128, 8], F32)
    x32 = [sb("x32a", [128, 4096], F32), sb("x32b", [128, 4096], F32)]
    xb = sb("xb", [128, 4096], BF16)
    Qb = sb("Qb", [128, 8192], BF16)
    Kr = [sb("k0", [128, 4096], BF16), sb("k1", [128, 4096], BF16)]
    Vr = [sb("v0", [128, 4160], BF16), sb("v1", [128, 4160], BF16)]
    Pt = [sb("p0", [128, 640], BF16), sb("p1", [128, 640], BF16),
          sb("p2", [128, 640], BF16), sb("p3", [128, 640], BF16)]
    Ofm = sb("ofm", [128, 4096], BF16)
    Otm = xb
    X1 = sb("x1tm", [128, 4096], F32)
    Q8 = sb("q8", [128, 4096], FP8)
    OutS = [sb("os0", [128, 1024], F32), sb("os1", [128, 1024], F32)]
    Sq = sb("sq", [128, 1024], BF16)
    Rws = sb("rws", [128, 64], F32)
    Scrp = sb("scrp", [128, 16], F32)
    Rinv = sb("rinv", [128, 64], F32)

    rot = [psum("rot0", [128, 512], F32), psum("rot1", [128, 512], F32)]
    S0 = psum("s0", [128, 1024], F32)
    S1 = psum("s1", [128, 1024], F32)
    LNt = psum("lnt", [128, 1024], F32)

    P = Prog()
    b_w = {n: Buf() for n in ("wq", "wk", "wv", "wo", "wf8", "biasT", "idn",
                              "idnb", "masks", "rowsC")}
    b_x32 = [Buf(), Buf()]
    b_xb = Buf()
    b_Otm = b_xb
    b_Q = Buf()
    b_K = [Buf(), Buf()]
    b_V = [Buf(), Buf()]
    b_Pt = [Buf(), Buf(), Buf(), Buf()]
    b_Ofm = Buf()
    b_X1 = Buf()
    b_Q8 = Buf()
    b_OutS = [Buf(), Buf()]
    b_Sq = Buf()
    b_Rws = Buf()
    b_Rinv = Buf()
    b_rot = [Buf(), Buf()]
    b_S = [Buf(), Buf()]
    b_LNt = Buf()

    WSB = {"q": wq, "k": wk, "v": wv, "o": wo}

    for nm, dram in (("wq", wq_d), ("wk", wk_d), ("wv", wv_d), ("wo", wo_d)):
        dst = WSB[nm[1]]
        P.add("gp", lambda e, dst=dst, dram=dram: e.dma_start(
            out=dst[:, :],
            in_=ap(dram, 0, [[C, 128], [128 * C, 8], [1, C]])),
            writes=[b_w[nm]], cls="gp_pre")
    P.add("sp", lambda e: e.dma_start(out=biasT[:, :],
          in_=ap(bias_d, 0, [[8, 128], [1, 8]])), writes=[b_w["biasT"]],
          cls="sp_pre")
    P.add("sp", lambda e: e.dma_start(out=idn[:, :],
          in_=ap(idn_d, 0, [[128, 128], [1, 128]])), writes=[b_w["idn"]],
          cls="sp_pre")
    P.add("sp", lambda e: e.dma_start(out=masks[:, :],
          in_=ap(msk_d, 0, [[256, 128], [1, 256]])), writes=[b_w["masks"]],
          cls="sp_pre")
    P.add("dve", lambda e: e.tensor_copy(idnb[:, :], idn[:, :]),
          reads=[b_w["idn"]], writes=[b_w["idnb"]])
    for col, val in ((0, 448.0), (1, -448.0), (2, 1e-5), (3, 1.0 / 1024.0),
                     (4, -12.0), (5, -1.0)):
        P.add("dve", lambda e, col=col, val=val:
              e.memset(rowsC[:, col:col + 1], val), writes=[b_w["rowsC"]])
    P.add("dve", lambda e: e.memset(Qb[:, :], 0.0), writes=[b_Q])
    for fn in (AF.Exp, AF.Square, AF.Sqrt, AF.Identity):
        P.add("act", lambda e, fn=fn: e.activation(
            Scrp[:, 0:8], rowsC[:, 0:8], fn),
            reads=[b_w["rowsC"]])
    P.add("act", lambda e: e.copy(Scrp[:, 0:8], rowsC[:, 0:8]),
          reads=[b_w["rowsC"]])
    for sl in range(2):
        P.add("dve", lambda e, sl=sl: e.memset(
            ap(Vr[sl], 64, [[4160, 128], [1040, 4], [65, 16]]), 1.0),
            writes=[b_V[sl]])
    for half in range(2):
        P.add("sp", lambda e, half=half: e.dma_start(
            out=x32[half][:, :],
            in_=ap(wf_d, half * 512 * C, [[C, 128], [128 * C, 4], [1, C]])),
            writes=[b_x32[half]], cls="sp_pre")
        for kc in range(4):
            kk = half * 4 + kc
            P.add("dve", lambda e, half=half, kc=kc, kk=kk: e.tensor_scalar(
                wf8[:, kk * 1024:(kk + 1) * 1024],
                x32[half][:, kc * 1024:(kc + 1) * 1024],
                rowsC[:, 0:1], rowsC[:, 1:2], ALU.min, ALU.max),
                reads=[b_x32[half], b_w["rowsC"]], writes=[b_w["wf8"]])

    P.scratch = Scrp
    P.mark_prelude()

    rot_i = [0]

    def claim():
        r = rot_i[0] & 1
        rot_i[0] += 1
        return r

    def spacer(bufs):
        w = list(bufs)
        if b_S[1] not in w:
            w.append(b_S[1])
        P.add("pe", lambda e: e.matmul(
            S1[0:1, 1023:1024], idnb[:, 0:1], idnb[:, 0:1],
            start=True, stop=True), writes=w)

    def spacer_t(bufs):
        P.add("pe", lambda e: e.transpose(
            S1[0:1, 768:896], idn[:, 0:1], idn[:, :]),
            reads=[b_w["idn"]], writes=[b_S[1]])
        spacer(bufs)

    def dve_gap(n=3):
        for _ in range(n):
            P.add("dve", lambda e: e.tensor_copy(
                Scrp[0:1, 10:11], Scrp[0:1, 11:12]))

    def ab_parts(tile):
        s, tt = divmod(tile, NTL)
        p = tile % 2
        vsl = tt % 2
        xoff = s * C * T + tt * TT
        P.add("sp", lambda e, xoff=xoff, p=p: e.dma_start(
            out=x32[p][:, :],
            in_=ap(xT, xoff, [[T, 128], [128 * T, 8], [1, TT]])),
            writes=[b_x32[p]], cls="sp_x")
        P.add("gp", lambda e, xoff=xoff: e.dma_start(
            out=xb[:, :],
            in_=ap(xT, xoff, [[T, 128], [128 * T, 8], [1, TT]])),
            writes=[b_xb], cls="gp_x")
        parts = []
        for nm, dstbuf, dst in (("q", b_Q, Qb), ("k", b_K[tt % 2], Kr[tt % 2])):
            for j in range(8):
                def part(nm=nm, dstbuf=dstbuf, dst=dst, j=j):
                    r = claim()
                    for kc in range(8):
                        P.add("pe", lambda e, r=r, nm=nm, j=j, kc=kc:
                              e.matmul(rot[r][:, :],
                                       WSB[nm][:, kc * 1024 + j * 128:
                                               kc * 1024 + j * 128 + 128],
                                       xb[:, kc * 512:(kc + 1) * 512],
                                       start=(kc == 0), stop=(kc == 7)),
                              reads=[b_w["w" + nm], b_xb], writes=[b_rot[r]])
                    spacer([b_rot[r]])
                    if nm == "k":
                        P.add("dve", lambda e, r=r, dst=dst, j=j: e.tensor_copy(
                            dst[:, j * 512:(j + 1) * 512], rot[r][:, :]),
                            reads=[b_rot[r]], writes=[dstbuf])
                    else:
                        P.add("dve", lambda e, r=r, dst=dst, j=j: e.tensor_copy(
                            dst[0:64, (2 * j) * 512:(2 * j + 1) * 512],
                            rot[r][0:64, :]),
                            reads=[b_rot[r]], writes=[dstbuf])
                        P.add("dve", lambda e, r=r, dst=dst, j=j: e.tensor_copy(
                            dst[64:128, (2 * j + 1) * 512:(2 * j + 2) * 512],
                            rot[r][64:128, :]),
                            reads=[b_rot[r]], writes=[dstbuf])
                parts.append(part)
        for ts in range(4):
            for fh in range(2):
                def part(ts=ts, fh=fh, vsl=vsl):
                    r = claim()
                    for kc in range(8):
                        P.add("pe", lambda e, r=r, ts=ts, fh=fh, kc=kc:
                              e.matmul(rot[r][:, :],
                                       xb[:, kc * 512 + ts * 128:
                                          kc * 512 + ts * 128 + 128],
                                       wv[:, kc * 1024 + fh * 512:
                                          kc * 1024 + fh * 512 + 512],
                                       start=(kc == 0), stop=(kc == 7)),
                              reads=[b_w["wv"], b_xb], writes=[b_rot[r]])
                    spacer([b_rot[r]])
                    P.add("dve", lambda e, r=r, vsl=vsl, ts=ts, fh=fh:
                          e.tensor_copy(
                              ap(Vr[vsl], ts * 1040 + fh * 520,
                                 [[4160, 128], [65, 8], [1, 64]]),
                              rot[r][:, :]),
                          reads=[b_rot[r]], writes=[b_V[vsl]])
                parts.append(part)
        return parts

    pending = []

    def pop_pending(n):
        for _ in range(n):
            if pending:
                pending.pop(0)()

    n_out_dma = 0
    last_out_op = None
    for tile in range(BPC * NTL):
        s, tt = divmod(tile, NTL)
        p = tile % 2
        vsl = tt % 2
        if tile == 0:
            for f in ab_parts(0):
                f()

        def emit_S(h, qts):
            binfo = {qt: band_chunks(tt, qt) for qt in qts}
            union = sorted({k for qt in qts for k in binfo[qt][0]})
            fc = h // 2
            for kabs in union:
                ksl = (kabs // TT) % 2
                kcol = kabs % TT
                for qt in qts:
                    chunks, q0, bstart = binfo[qt]
                    if kabs not in chunks:
                        continue
                    crel = (kabs - bstart) // QT
                    sreg = S0 if qt % 2 == 0 else S1
                    P.add("pe", lambda e, ksl=ksl, kcol=kcol, h=h,
                          fc=fc, qt=qt, crel=crel, sreg=sreg:
                          e.matmul(sreg[:, crel * 128:(crel + 1) * 128],
                                   Kr[ksl][:, fc * 512 + kcol:
                                           fc * 512 + kcol + 128],
                                   Qb[:, h * 512 + qt * 128:
                                      h * 512 + qt * 128 + 128],
                                   start=True, stop=True),
                          reads=[b_K[ksl], b_Q],
                          writes=[b_S[qt % 2]])
            spacer([b_S[0], b_S[1]])
            for qt in qts:
                chunks, q0, bstart = binfo[qt]
                n = len(chunks)
                sreg = S0 if qt % 2 == 0 else S1
                P.add("act", lambda e, sreg=sreg, n=n, qt=qt:
                      e.activation(Pt[qt][:, 0:n * 128],
                                   sreg[:, 0:n * 128], AF.Exp,
                                   bias=rowsC[:, 4:5], scale=0.125),
                      reads=[b_S[qt % 2], b_w["rowsC"]],
                      writes=[b_Pt[qt]])
                if q0 >= WIN:
                    P.add("dve", lambda e, qt=qt, n=n: e.tensor_tensor(
                        ap(Pt[qt], 0, [[640, 128], [(n - 1) * 128, 2],
                                       [1, 128]]),
                        ap(Pt[qt], 0, [[640, 128], [(n - 1) * 128, 2],
                                       [1, 128]]),
                        ap(masks, 0, [[256, 128], [128, 2], [1, 128]]),
                        ALU.mult),
                        reads=[b_w["masks"]], writes=[b_Pt[qt]])
                else:
                    P.add("dve", lambda e, qt=qt, n=n: e.tensor_tensor(
                        Pt[qt][:, (n - 1) * 128:n * 128],
                        Pt[qt][:, (n - 1) * 128:n * 128],
                        masks[:, 128:256], ALU.mult),
                        reads=[b_w["masks"]], writes=[b_Pt[qt]])

        def emit_PV(h, qts):
            pr = h % 2
            for qt in qts:
                chunks, q0, bstart = band_chunks(tt, qt)
                n = len(chunks)
                for ci, kabs in enumerate(chunks):
                    ksl = (kabs // TT) % 2
                    ts = (kabs % TT) // 128
                    P.add("pe", lambda e, qt=qt, ci=ci, n=n, ksl=ksl,
                          ts=ts, h=h, pr=pr:
                          e.matmul(rot[pr][:, qt * 65:qt * 65 + 65],
                                   Pt[qt][:, ci * 128:(ci + 1) * 128],
                                   Vr[ksl][:, ts * 1040 + h * 65:
                                           ts * 1040 + h * 65 + 65],
                                   start=(ci == 0), stop=(ci == n - 1)),
                          reads=[b_Pt[qt], b_V[ksl]],
                          writes=[b_rot[pr]])

        def emit_Orecip(h):
            pr = h % 2
            spacer([b_rot[pr]])
            P.add("dve", lambda e, pr=pr, h=h: e.reciprocal(
                Rinv[:, h * 4:h * 4 + 4],
                ap(rot[pr], 64, [[512, 128], [65, 4]])),
                reads=[b_rot[pr]], writes=[b_Rinv])

        def emit_Oevac(h):
            pr = h % 2
            for qt in range(4):
                P.add("dve", lambda e, pr=pr, h=h, qt=qt: e.tensor_scalar_mul(
                    ap(Otm, qt * 1024 + h * 64, [[4096, 128], [1, 64]]),
                    ap(rot[pr], qt * 65, [[512, 128], [1, 64]]),
                    Rinv[:, h * 4 + qt:h * 4 + qt + 1]),
                    reads=[b_rot[pr], b_Rinv], writes=[b_Otm])

        emit_S(0, [0, 1])
        emit_S(0, [2, 3])
        emit_PV(0, [0, 1])
        for h in range(1, 16):
            emit_S(h, [0, 1])
            emit_PV(h - 1, [2, 3])
            emit_Orecip(h - 1)
            if h >= 2:
                emit_Oevac(h - 2)
            emit_S(h, [2, 3])
            emit_PV(h, [0, 1])
        emit_PV(15, [2, 3])
        emit_Orecip(15)
        dve_gap()
        emit_Oevac(14)
        emit_Oevac(15)

        for qt in range(4):
            for half in range(2):
                r = claim()
                for f4 in range(4):
                    fcc = half * 4 + f4
                    P.add("pe", lambda e, r=r, qt=qt, fcc=fcc, f4=f4:
                          e.matmul(rot[r][:, f4 * 128:(f4 + 1) * 128],
                                   Otm[:, qt * 1024 + fcc * 128:
                                       qt * 1024 + fcc * 128 + 128],
                                   idnb[:, :], start=True, stop=True),
                          reads=[b_Otm, b_w["idnb"]], writes=[b_rot[r]])
                spacer([b_rot[r]])
                P.add("dve", lambda e, r=r, qt=qt, half=half: e.tensor_copy(
                    ap(Ofm, half * 4 * 512 + qt * 128,
                       [[4096, 128], [512, 4], [1, 128]]),
                    rot[r][:, :]),
                    reads=[b_rot[r]], writes=[b_Ofm])

        for j in range(8):
            r = claim()
            for kc in range(8):
                P.add("pe", lambda e, r=r, j=j, kc=kc:
                      e.matmul(rot[r][:, :],
                               wo[:, kc * 1024 + j * 128:
                                  kc * 1024 + j * 128 + 128],
                               Ofm[:, kc * 512:(kc + 1) * 512],
                               start=(kc == 0), stop=(kc == 7)),
                      reads=[b_w["wo"], b_Ofm], writes=[b_rot[r]])
            spacer([b_rot[r]])
            P.add("dve", lambda e, r=r, j=j, p=p: e.tensor_add(
                x32[p][:, j * 512:(j + 1) * 512], rot[r][:, :],
                x32[p][:, j * 512:(j + 1) * 512]),
                reads=[b_rot[r]], writes=[b_x32[p]])

        pending[:] = ab_parts(tile + 1) if tile + 1 < BPC * NTL else []

        for sub in range(4):
            for fcc in range(8):
                P.add("pe", lambda e, fcc=fcc, sub=sub, p=p: e.transpose(
                    LNt[:, fcc * 128:(fcc + 1) * 128],
                    x32[p][:, fcc * 512 + sub * 128:
                           fcc * 512 + sub * 128 + 128],
                    idn[:, :]),
                    reads=[b_x32[p], b_w["idn"]], writes=[b_LNt])
            spacer_t([b_LNt])
            c0 = sub * 8
            red_op = P.add("dve", lambda e, c0=c0: e.reduce_sum(
                Rws[:, c0:c0 + 1], LNt[:, :], axis=AX.X),
                reads=[b_LNt], writes=[b_Rws])
            P.add("act", lambda e, c0=c0: e.activation(
                Sq[:, :], LNt[:, :], AF.Square),
                reads=[b_LNt], writes=[b_Sq], extra=[red_op])
            P.add("dve", lambda e, c0=c0: e.reduce_sum(
                Rws[:, c0 + 1:c0 + 2], Sq[:, :], axis=AX.X),
                reads=[b_Sq], writes=[b_Rws])
            P.add("dve", lambda e, c0=c0: e.tensor_scalar_mul(
                Rws[:, c0 + 2:c0 + 3], Rws[:, c0:c0 + 1], rowsC[:, 3:4]),
                reads=[b_w["rowsC"]], writes=[b_Rws])
            dve_gap()
            P.add("dve", lambda e, c0=c0: e.tensor_mul(
                Rws[:, c0 + 5:c0 + 6], Rws[:, c0 + 2:c0 + 3],
                Rws[:, c0 + 2:c0 + 3]), writes=[b_Rws])
            dve_gap()
            P.add("dve", lambda e, c0=c0: e.scalar_tensor_tensor(
                Rws[:, c0 + 6:c0 + 7], Rws[:, c0 + 1:c0 + 2], rowsC[:, 3:4],
                Rws[:, c0 + 5:c0 + 6], ALU.mult, ALU.subtract),
                reads=[b_w["rowsC"]], writes=[b_Rws])
            P.add("act", lambda e, c0=c0: e.activation(
                Rws[:, c0 + 7:c0 + 8], Rws[:, c0 + 6:c0 + 7], AF.Sqrt,
                bias=rowsC[:, 2:3]),
                reads=[b_w["rowsC"]], writes=[b_Rws])
            P.add("dve", lambda e, c0=c0: e.reciprocal(
                Rws[:, c0 + 3:c0 + 4], Rws[:, c0 + 7:c0 + 8]),
                writes=[b_Rws])
            dve_gap()
            P.add("dve", lambda e, c0=c0: e.scalar_tensor_tensor(
                Rws[:, c0 + 4:c0 + 5], Rws[:, c0 + 2:c0 + 3], rowsC[:, 5:6],
                Rws[:, c0 + 3:c0 + 4], ALU.mult, ALU.mult),
                reads=[b_w["rowsC"]], writes=[b_Rws])
            P.add("act", lambda e, c0=c0, sub=sub: e.activation(
                X1[:, sub * 1024:(sub + 1) * 1024], LNt[:, :], AF.Identity,
                bias=Rws[:, c0 + 4:c0 + 5], scale=Rws[:, c0 + 3:c0 + 4]),
                reads=[b_LNt, b_Rws], writes=[b_X1])
            pop_pending(4)

        for sub in range(4):
            for half in range(2):
                r = claim()
                for f4 in range(4):
                    fcc = half * 4 + f4
                    P.add("pe", lambda e, r=r, f4=f4, fcc=fcc, sub=sub:
                          e.transpose(rot[r][:, f4 * 128:(f4 + 1) * 128],
                                      X1[:, sub * 1024 + fcc * 128:
                                         sub * 1024 + fcc * 128 + 128],
                                      idn[:, :]),
                          reads=[b_X1, b_w["idn"]], writes=[b_rot[r]])
                spacer_t([b_rot[r]])
                P.add("dve", lambda e, r=r, half=half, sub=sub: e.tensor_copy(
                    ap(Q8, half * 4 * 512 + sub * 128,
                       [[4096, 128], [512, 4], [1, 128]]),
                    rot[r][:, :]),
                    reads=[b_rot[r]], writes=[b_Q8])

        for j in range(8):
            r = claim()
            for kc in range(8):
                P.add("pe", lambda e, r=r, j=j, kc=kc:
                      e.matmul(rot[r][:, :],
                               wf8[:, kc * 1024 + j * 128:
                                   kc * 1024 + j * 128 + 128],
                               Q8[:, kc * 512:(kc + 1) * 512],
                               start=(kc == 0), stop=(kc == 7)),
                      reads=[b_w["wf8"], b_Q8], writes=[b_rot[r]])
            spacer([b_rot[r]])
            P.add("dve", lambda e, r=r, j=j: e.tensor_scalar_add(
                Ofm[:, j * 512:(j + 1) * 512], rot[r][:, :],
                biasT[:, j:j + 1]),
                reads=[b_rot[r], b_w["biasT"]], writes=[b_Ofm])

        for sub in range(4):
            for j in range(8):
                P.add("pe", lambda e, j=j, sub=sub: e.matmul(
                    LNt[:, j * 128:(j + 1) * 128],
                    Ofm[:, j * 512 + sub * 128:j * 512 + sub * 128 + 128],
                    idnb[:, :], start=True, stop=True),
                    reads=[b_Ofm, b_w["idnb"]], writes=[b_LNt])
            spacer([b_LNt])
            P.add("dve", lambda e, sub=sub: e.tensor_add(
                X1[:, sub * 1024:(sub + 1) * 1024], LNt[:, :],
                X1[:, sub * 1024:(sub + 1) * 1024]),
                reads=[b_LNt], writes=[b_X1])
            c0 = sub * 8 + 32
            P.add("dve", lambda e, c0=c0, sub=sub: e.reduce_sum(
                Rws[:, c0:c0 + 1], X1[:, sub * 1024:(sub + 1) * 1024],
                axis=AX.X),
                reads=[b_X1], writes=[b_Rws])
            P.add("act", lambda e, c0=c0, sub=sub: e.activation(
                Sq[:, :], X1[:, sub * 1024:(sub + 1) * 1024], AF.Square),
                reads=[b_X1], writes=[b_Sq])
            P.add("dve", lambda e, c0=c0: e.reduce_sum(
                Rws[:, c0 + 1:c0 + 2], Sq[:, :], axis=AX.X),
                reads=[b_Sq], writes=[b_Rws])
            P.add("dve", lambda e, c0=c0: e.tensor_scalar_mul(
                Rws[:, c0 + 2:c0 + 3], Rws[:, c0:c0 + 1], rowsC[:, 3:4]),
                reads=[b_w["rowsC"]], writes=[b_Rws])
            dve_gap()
            P.add("dve", lambda e, c0=c0: e.tensor_mul(
                Rws[:, c0 + 5:c0 + 6], Rws[:, c0 + 2:c0 + 3],
                Rws[:, c0 + 2:c0 + 3]), writes=[b_Rws])
            dve_gap()
            P.add("dve", lambda e, c0=c0: e.scalar_tensor_tensor(
                Rws[:, c0 + 6:c0 + 7], Rws[:, c0 + 1:c0 + 2], rowsC[:, 3:4],
                Rws[:, c0 + 5:c0 + 6], ALU.mult, ALU.subtract),
                reads=[b_w["rowsC"]], writes=[b_Rws])
            P.add("act", lambda e, c0=c0: e.activation(
                Rws[:, c0 + 7:c0 + 8], Rws[:, c0 + 6:c0 + 7], AF.Sqrt,
                bias=rowsC[:, 2:3]),
                reads=[b_w["rowsC"]], writes=[b_Rws])
            P.add("dve", lambda e, c0=c0: e.reciprocal(
                Rws[:, c0 + 3:c0 + 4], Rws[:, c0 + 7:c0 + 8]),
                writes=[b_Rws])
            dve_gap()
            P.add("dve", lambda e, c0=c0: e.scalar_tensor_tensor(
                Rws[:, c0 + 4:c0 + 5], Rws[:, c0 + 2:c0 + 3], rowsC[:, 5:6],
                Rws[:, c0 + 3:c0 + 4], ALU.mult, ALU.mult),
                reads=[b_w["rowsC"]], writes=[b_Rws])
            pop_pending(2)
            o = sub % 2
            P.add("act", lambda e, c0=c0, sub=sub, o=o: e.activation(
                OutS[o][:, :], X1[:, sub * 1024:(sub + 1) * 1024],
                AF.Identity, bias=Rws[:, c0 + 4:c0 + 5],
                scale=Rws[:, c0 + 3:c0 + 4]),
                reads=[b_X1, b_Rws], writes=[b_OutS[o]])
            ooff = (s * T + tt * TT + sub * 128) * C
            last_out_op = P.add("sp", lambda e, o=o, ooff=ooff: e.dma_start(
                out=ap(out_d, ooff, [[C, 128], [1, C]]), in_=OutS[o][:, :]),
                reads=[b_OutS[o]], cls="sp_out")
            n_out_dma += 1
        pop_pending(99)

    P.add("sp", lambda e: e.nop(), extra=[last_out_op])

    P.finalize()

    with st:
        with (
            nc.semaphore("s_pe") as s_pe,
            nc.semaphore("s_act") as s_act,
            nc.semaphore("s_dve") as s_dve,
            nc.semaphore("s_sp") as s_sp,
            nc.semaphore("s_gp") as s_gp,
            nc.semaphore("s_spp") as s_spp,
            nc.semaphore("s_spx") as s_spx,
            nc.semaphore("s_spo") as s_spo,
            nc.semaphore("s_gpp") as s_gpp,
            nc.semaphore("s_gpx") as s_gpx,
            nc.Block() as block,
        ):
            sems = {"pe": s_pe, "act": s_act, "dve": s_dve,
                    "sp": s_sp, "gp": s_gp,
                    "sp_pre": s_spp, "sp_x": s_spx, "sp_out": s_spo,
                    "gp_pre": s_gpp, "gp_x": s_gpx}

            @block.tensor
            def _(e):
                P.replay("pe", e, sems)

            @block.scalar
            def _(e):
                P.replay("act", e, sems)

            @block.vector
            def _(e):
                P.replay("dve", e, sems)

            @block.sync
            def _(e):
                P.replay("sp", e, sems)

            @block.gpsimd
            def _(e):
                P.replay("gp", e, sems)

    return nc


def kernel(L0_attn_x, L0_attn_Wq, L0_attn_Wk, L0_attn_Wv, L0_attn_Wo,
           L0_ffn_weight, L0_ffn_bias):
    x = np.asarray(L0_attn_x, dtype=np.float32)
    wqT = np.ascontiguousarray(np.asarray(L0_attn_Wq, dtype=np.float32).T)
    wkT = np.ascontiguousarray(np.asarray(L0_attn_Wk, dtype=np.float32).T)
    wvT = np.ascontiguousarray(np.asarray(L0_attn_Wv, dtype=np.float32).T)
    woT = np.ascontiguousarray(np.asarray(L0_attn_Wo, dtype=np.float32).T)
    wfT = np.ascontiguousarray(np.asarray(L0_ffn_weight, dtype=np.float32).T)
    biasT = np.ascontiguousarray(
        np.asarray(L0_ffn_bias, dtype=np.float32).reshape(8, 128).T)
    idn = np.eye(128, dtype=np.float32)
    pq = np.arange(128)[:, None]
    qq = np.arange(128)[None, :]
    m0 = (qq < pq)
    m4 = (qq >= pq)
    masks = np.concatenate([m0, m4], axis=1).astype(ml_dtypes.bfloat16)

    xT = np.ascontiguousarray(x.transpose(0, 2, 1))

    nc = build_program()
    in_maps = []
    for core in range(NCORES):
        shard = np.ascontiguousarray(
            xT[core * BPC:(core + 1) * BPC].reshape(BPC * C, T))
        in_maps.append({
            "xT": shard, "wqT": wqT, "wkT": wkT, "wvT": wvT, "woT": woT,
            "wfT": wfT, "biasT": biasT, "idn": idn, "masks": masks,
        })
    res = run_bass_kernel_spmd(nc, in_maps, core_ids=list(range(NCORES)))
    out = np.concatenate(
        [r["out"].reshape(BPC, T, C) for r in res.results], axis=0)
    return np.ascontiguousarray(out.astype(np.float32))
